# version 01
# baseline (speedup 1.0000x reference)
import numpy as np
from contextlib import ExitStack

import concourse.bass as bass
import concourse.mybir as mybir
from concourse.bass_utils import run_bass_kernel_spmd

F32 = mybir.dt.float32
BF16 = mybir.dt.bfloat16
AF = mybir.ActivationFunctionType
ALU = mybir.AluOpType
AX = mybir.AxisListType

S = 2048
D = 1024
KVW = 512
NQH = 16
TC = 16
CC = 128

C_EXPB = 0
C_EPS = 1


class Sem:
    def __init__(self, st, nc, name):
        self.h = st.enter_context(nc.semaphore(name))
        self.n = 0

    def inc(self, instr, k=1):
        instr.then_inc(self.h, k)
        self.n += k
        return self.n


def build(upto=99, taps=(), dbg_dram=False):
    nc = bass.Bass("TRN2", target_bir_lowering=False)
    taps = set(taps)
    SCR = "ExternalOutput" if dbg_dram else "Internal"

    x = nc.dram_tensor("x", [S, D], F32, kind="ExternalInput").ap()
    wq = nc.dram_tensor("wq", [D, D], F32, kind="ExternalInput").ap()
    wk = nc.dram_tensor("wk", [D, KVW], F32, kind="ExternalInput").ap()
    wv = nc.dram_tensor("wv", [D, KVW], F32, kind="ExternalInput").ap()
    mem = nc.dram_tensor("mem", [KVW, KVW], F32, kind="ExternalInput").ap()
    memn = nc.dram_tensor("memn", [KVW], F32, kind="ExternalInput").ap()
    memw = nc.dram_tensor("memw", [1], F32, kind="ExternalInput").ap()
    rm = nc.dram_tensor("rm", [4 * 64 * 16, 16], F32, kind="ExternalInput").ap()
    rup = nc.dram_tensor("rup", [4 * 64 * 64, 16], F32, kind="ExternalInput").ap()
    wc = nc.dram_tensor("wc", [D, 16384], F32, kind="ExternalInput").ap()
    cst = nc.dram_tensor("cst", [128, 8], F32, kind="ExternalInput").ap()
    idin = nc.dram_tensor("idin", [128, 128], F32, kind="ExternalInput").ap()
    out = nc.dram_tensor("out", [S, D], F32, kind="ExternalOutput").ap()

    x_bf = nc.dram_tensor("x_bf", [S, D], BF16, kind=SCR).ap()
    wc_bf = nc.dram_tensor("wc_bf", [D, 16384], BF16, kind=SCR).ap()
    x1d = nc.dram_tensor("x1d", [S, D], F32, kind=SCR).ap()
    x1_bf = nc.dram_tensor("x1_bf", [S, D], BF16, kind=SCR).ap()
    h_dram = nc.dram_tensor("h_dram", [CC, 128, S], BF16, kind=SCR).ap()

    tap_aps = {}
    for name, shp in [("q_fm", [D, S]), ("k_fm", [KVW, S]), ("sg_fm", [D, S]),
                      ("v_aug", [S, 8 * 65]),
                      ("attn1", [S, D]), ("attn2", [S, D]), ("x1", [S, D]),
                      ("hh", [CC, 128, S]), ("lz", [128, (CC // 2) * 128])]:
        if name in taps:
            tap_aps[name] = nc.dram_tensor(
                "tap_" + name, shp, F32, kind="ExternalOutput").ap()

    with ExitStack() as st:
        E = st.enter_context

        cst_sb = E(nc.sbuf_tensor("cst_sb", [128, 8], F32))
        ident = E(nc.sbuf_tensor("ident", [128, 128], BF16))
        arena1 = E(nc.sbuf_tensor("arena1", [128, TC * D], F32))
        acc = arena1
        x_fm = arena1[:, 0:8192].bitcast(BF16)
        wq_sb = arena1[:, 8192:12288].bitcast(BF16)
        wk_sb = arena1[:, 12288:14336].bitcast(BF16)
        wv_sb = arena1[:, 14336:16384].bitcast(BF16)
        mem_sb = E(nc.sbuf_tensor("mem_sb", [128, 4 * KVW], BF16))
        memn_sb = E(nc.sbuf_tensor("memn_sb", [128, 4], BF16))
        q_fm = E(nc.sbuf_tensor("q_fm", [128, 8 * S], BF16))
        x1t = [q_fm[:, 0:2048].bitcast(F32), q_fm[:, 2048:4096].bitcast(F32)]
        outt = [q_fm[:, 4096:6144].bitcast(F32), q_fm[:, 6144:8192].bitcast(F32)]
        sq_scr = q_fm[:, 8192:10240].bitcast(F32)
        sg_fm = E(nc.sbuf_tensor("sg_fm", [128, 8 * S], BF16))
        x1fm = [sg_fm[:, 0:2048], sg_fm[:, 2048:4096]]
        hstage = [sg_fm[:, 4096:6144], sg_fm[:, 6144:8192]]
        wt_t = [sg_fm[:, 8192:8704], sg_fm[:, 8704:9216]]
        h_t = [sg_fm[:, 9216:9728], sg_fm[:, 9728:10240]]
        k_fm = E(nc.sbuf_tensor("k_fm", [128, 4 * S], BF16))
        v_aug = E(nc.sbuf_tensor("v_aug", [128, TC * 520], BF16))
        pb = [E(nc.sbuf_tensor(f"pb{i}", [128, S], BF16)) for i in range(2)]
        xin = [pb[0][:, :].bitcast(F32), pb[1][:, :].bitcast(F32)]
        dens = E(nc.sbuf_tensor("dens", [128, TC * NQH], F32))
        w_col = E(nc.sbuf_tensor("w_col", [128, 4], F32))
        tmp_a = E(nc.sbuf_tensor("tmp_a", [128, 512], F32))
        tmp_b = E(nc.sbuf_tensor("tmp_b", [128, 512], F32))
        tmp_c = E(nc.sbuf_tensor("tmp_c", [128, 512], F32))
        dtmp = E(nc.sbuf_tensor("dtmp", [128, 16], F32))
        db = E(nc.sbuf_tensor("db", [128, 16 * 64], F32))
        dscr = E(nc.sbuf_tensor("dscr", [128, 512], F32))
        stats = E(nc.sbuf_tensor("stats", [128, 8 * 16], F32))
        lz = E(nc.sbuf_tensor("lz", [128, (CC // 2) * 128], BF16))
        lzstg = E(nc.sbuf_tensor("lzstg", [32, 2 * 128], BF16))
        mt_sb = v_aug[0:16, 0:4096]
        ut_stg = E(nc.sbuf_tensor("ut_stg", [16, 2 * 128], BF16))
        rm_sb = dscr[:, 0:256].bitcast(BF16)
        rup_sb = db[:, :].bitcast(BF16)

        ps = E(nc.psum_tensor("ps", [128, 4096], F32))
        PS_SC = 0
        PS_PV = 2048
        PS_B = 3072
        PS_A = 3584
        PS_Z = 2048
        PS_CP = 0

        def pv_off(k):
            return PS_PV + 512 * (k // 7) + 73 * (k % 7)

        def qloc(n):
            j = n // 2
            if j % 2 == 0:
                return j + (n % 2), 0
            return j - 1 + (n % 2), 64

        def lzloc(ci):
            u = ci % 32
            half = 64 * ((u % 4) // 2)
            slot = (ci // 4) * 2 + (u % 4) % 2
            return half, slot

        names = ["ld", "ldw", "ldx", "mm", "exp", "pv", "pvev", "dv", "ac",
                 "dst", "dh", "dx1", "dxb", "gl", "zmm", "cp", "ev", "ou", "po",
                 "tp", "xi", "ft", "wt", "ht"]
        sm = {n: Sem(st, nc, n) for n in names}
        ld = sm["ld"]; ldw = sm["ldw"]; ldx = sm["ldx"]
        s_mm = sm["mm"]; s_exp = sm["exp"]; s_pv = sm["pv"]; s_pvev = sm["pvev"]
        s_dv = sm["dv"]; s_ac = sm["ac"]; s_dst = sm["dst"]; s_dh = sm["dh"]
        s_dx1 = sm["dx1"]; s_dxb = sm["dxb"]; s_gl = sm["gl"]; s_zmm = sm["zmm"]; s_cp = sm["cp"]
        s_ev = sm["ev"]; s_ou = sm["ou"]; s_po = sm["po"]; s_tp = sm["tp"]
        s_xi = sm["xi"]; s_ft = sm["ft"]; s_wt = sm["wt"]; s_ht = sm["ht"]

        def dve_sync():
            nc.vector.wait_ge(s_dv.h, s_dv.n)

        for j in range(8):
            ld.inc(nc.gpsimd.dma_start(wc_bf[128 * j:128 * (j + 1), :],
                                       wc[128 * j:128 * (j + 1), :]), 16)
        WC_DONE = ld.n

        ldx.inc(nc.gpsimd.dma_start(x_bf, x), 16)
        ldw.inc(nc.gpsimd.dma_start(cst_sb[:, :], cst), 16)
        CST_LD = ldw.n
        ldw.inc(nc.gpsimd.dma_start(ident[:, :], idin), 16)
        ldw.inc(nc.gpsimd.dma_start(
            wq_sb[:, :].rearrange("p (g f) -> p g f", g=8),
            wq.rearrange("(g p) f -> p g f", p=128)), 16)
        ldw.inc(nc.gpsimd.dma_start(
            wk_sb[:, :].rearrange("p (g f) -> p g f", g=8),
            wk.rearrange("(g p) f -> p g f", p=128)), 16)
        ldw.inc(nc.gpsimd.dma_start(
            wv_sb[:, :].rearrange("p (g f) -> p g f", g=8),
            wv.rearrange("(g p) f -> p g f", p=128)), 16)
        ldw.inc(nc.gpsimd.dma_start(
            mem_sb[:, :].rearrange("p (g f) -> p g f", g=4),
            mem.rearrange("(g p) f -> p g f", p=128)), 16)
        ldw.inc(nc.gpsimd.dma_start(
            memn_sb[:, :], memn.rearrange("(g p) -> p g", p=128),
            allow_slow_non_contiguous=True), 16)
        for j in range(4):
            ldw.inc(nc.gpsimd.dma_start(
                rm_sb[:, j * 128:(j + 1) * 128].rearrange(
                    "p (t f) -> p t f", t=8),
                rm[j * 1024:(j + 1) * 1024, :].rearrange(
                    "(t p) f -> p t f", p=128)), 16)
        for j in range(16):
            ldw.inc(nc.gpsimd.dma_start(
                rup_sb[:, j * 128:(j + 1) * 128].rearrange(
                    "p (t f) -> p t f", t=8),
                rup[j * 1024:(j + 1) * 1024, :].rearrange(
                    "(t p) f -> p t f", p=128)), 16)
        W_SMALL = ldw.n

        ldw.inc(nc.gpsimd.dma_start(
            w_col[:, 2:3], memw[0:1].unsqueeze(0).broadcast_to([128, 1])), 16)
        MEMW_LD = ldw.n
        nc.scalar.wait_ge(ldw.h, MEMW_LD)
        s_ac.inc(nc.scalar.activation(w_col[:, 0:1], w_col[:, 2:3], AF.Sigmoid))
        W_SIG = s_ac.n
        nc.vector.wait_ge(s_ac.h, W_SIG)
        s_dv.inc(nc.vector.tensor_scalar(w_col[:, 1:2], w_col[:, 0:1], -1.0, 1.0,
                                         ALU.mult, ALU.add))
        W_COL_DV = s_dv.n

        nc.sync.wait_ge(ldx.h, ldx.n)
        for fc in range(8):
            for tb in range(4):
                s_tp.inc(nc.sync.dma_start(
                    x_fm[:, fc * S + tb * 512:fc * S + (tb + 1) * 512],
                    x_bf[tb * 512:(tb + 1) * 512, fc * 128:(fc + 1) * 128],
                    transpose=True), 16)
        XFM_DONE = s_tp.n

        s_dv.inc(nc.vector.memset(lz[:, :], 0.0))
        MEMSET_DONE = s_dv.n

        nc.tensor.wait_ge(ldw.h, W_SMALL)
        nc.gpsimd.wait_ge(s_tp.h, XFM_DONE)
        for t in range(32):
            s_mm.inc(nc.tensor.transpose(
                ps[0:16, PS_A:PS_A + 64].bitcast(BF16),
                rm_sb[:, t * 16:(t + 1) * 16], ident[:, :]))
            nc.vector.wait_ge(s_mm.h, s_mm.n)
            s_dv.inc(nc.vector.tensor_copy(
                mt_sb[0:16, t * 128:(t + 1) * 128],
                ps[0:16, PS_A:PS_A + 64].bitcast(BF16)))
            nc.tensor.wait_ge(s_dv.h, s_dv.n)
        nc.vector.wait_ge(s_dv.h, MEMSET_DONE)
        for ci in range(128):
            h4, u = ci // 32, ci % 32
            g = u // 4
            sl2 = ci % 2
            toff = (h4 * 8 + g) * 128
            s_mm.inc(nc.tensor.transpose(
                ps[0:16, PS_A:PS_A + 64].bitcast(BF16),
                rup_sb[:, ci * 16:(ci + 1) * 16], ident[:, :]))
            nc.vector.wait_ge(s_mm.h, s_mm.n)
            s_dv.inc(nc.vector.tensor_copy(
                ut_stg[0:16, sl2 * 128:(sl2 + 1) * 128],
                ps[0:16, PS_A:PS_A + 64].bitcast(BF16)))
            nc.tensor.wait_ge(s_dv.h, s_dv.n)
            s_mm.inc(nc.tensor.matmul(
                ps[0:32, PS_B:PS_B + 128],
                mt_sb[0:16, toff + 32 * (u % 4):toff + 32 * (u % 4) + 32],
                ut_stg[0:16, sl2 * 128:(sl2 + 1) * 128],
                start=True, stop=True))
            nc.vector.wait_ge(s_mm.h, s_mm.n)
            if ci >= 2:
                nc.vector.wait_ge(s_po.h, 32 * (ci - 1))
            s_dv.inc(nc.vector.tensor_copy(
                lzstg[0:32, sl2 * 128:(sl2 + 1) * 128],
                ps[0:32, PS_B:PS_B + 128]))
            nc.tensor.wait_ge(s_dv.h, s_dv.n)
            nc.gpsimd.wait_ge(s_dv.h, s_dv.n)
            lzh, lzslot = lzloc(ci)
            rb = lzh + 32 * (u % 2)
            cb = lzslot * 128
            s_po.inc(nc.gpsimd.dma_start(
                lz[rb:rb + 16, cb:cb + 64],
                lzstg[0:16, sl2 * 128:sl2 * 128 + 64]), 16)
            s_po.inc(nc.gpsimd.dma_start(
                lz[rb + 16:rb + 32, cb + 64:cb + 128],
                lzstg[16:32, sl2 * 128 + 64:sl2 * 128 + 128]), 16)
        LZ_DONE = s_po.n
        nc.vector.wait_ge(s_mm.h, s_mm.n)
        s_dv.inc(nc.vector.memset(v_aug[:, :], 1.0))
        VAUG_DONE = s_dv.n

        if "lz" in tap_aps:
            nc.gpsimd.wait_ge(s_po.h, LZ_DONE)
            s_dst.inc(nc.gpsimd.dma_start(tap_aps["lz"], lz[:, :]), 16)

        if upto >= 1:
            ev_marks = {}
            grp = 0
            nc.tensor.wait_ge(s_tp.h, XFM_DONE)
            for j in range(12):
                for tb in range(4):
                    bank = PS_A if grp % 2 == 0 else PS_B
                    if grp >= 2:
                        nc.tensor.wait_ge(s_dv.h, ev_marks[grp - 2])
                    for fi in range(8):
                        if j < 8:
                            lhsT = wq_sb[:, fi * D + j * 128:fi * D + (j + 1) * 128]
                        else:
                            jj = j - 8
                            lhsT = wk_sb[:, fi * KVW + jj * 128:
                                         fi * KVW + (jj + 1) * 128]
                        mmi = nc.tensor.matmul(
                            ps[:, bank:bank + 512], lhsT,
                            x_fm[:, fi * S + tb * 512:fi * S + (tb + 1) * 512],
                            start=(fi == 0), stop=(fi == 7))
                    s_mm.inc(mmi)
                    nc.vector.wait_ge(s_mm.h, s_mm.n)
                    if j < 8:
                        sl = slice(j * S + tb * 512, j * S + (tb + 1) * 512)
                        for hidx in range(2):
                            qc_, qb_ = qloc(2 * j + hidx)
                            s_dv.inc(nc.vector.tensor_copy(
                                q_fm[qb_:qb_ + 64,
                                     qc_ * S + tb * 512:qc_ * S + (tb + 1) * 512],
                                ps[64 * hidx:64 * hidx + 64, bank:bank + 512]))
                        s_dv.inc(nc.vector.tensor_scalar(
                            tmp_a[:, :], ps[:, bank:bank + 512], 0.0, None, ALU.min))
                        s_dv.inc(nc.vector.tensor_scalar(
                            tmp_b[:, :], ps[:, bank:bank + 512], 0.0, None, ALU.max))
                        nc.scalar.wait_ge(s_dv.h, s_dv.n)
                        s_ac.inc(nc.scalar.activation(tmp_c[:, :], tmp_a[:, :],
                                                      AF.Exp))
                        nc.vector.wait_ge(s_ac.h, s_ac.n)
                        s_dv.inc(nc.vector.scalar_tensor_tensor(
                            sg_fm[:, sl], tmp_c[:, :], -1.0, tmp_b[:, :],
                            ALU.add, ALU.add))
                        dve_sync()
                    else:
                        jj = j - 8
                        sl = slice(jj * S + tb * 512, jj * S + (tb + 1) * 512)
                        s_dv.inc(nc.vector.tensor_copy(k_fm[:, sl],
                                                       ps[:, bank:bank + 512]))
                    ev_marks[grp] = s_dv.n
                    grp += 1
            for tc in range(TC):
                bank = PS_A if grp % 2 == 0 else PS_B
                if grp >= 2:
                    nc.tensor.wait_ge(s_dv.h, ev_marks[grp - 2])
                for fi in range(8):
                    mmi = nc.tensor.matmul(
                        ps[:, bank:bank + 512],
                        x_fm[:, fi * S + tc * 128:fi * S + (tc + 1) * 128],
                        wv_sb[:, fi * KVW:(fi + 1) * KVW],
                        start=(fi == 0), stop=(fi == 7))
                s_mm.inc(mmi)
                nc.vector.wait_ge(s_mm.h, s_mm.n)
                s_dv.inc(nc.vector.tensor_copy(
                    v_aug[:, tc * 520:(tc + 1) * 520].rearrange(
                        "p (n c) -> p n c", n=8)[:, :, 0:64],
                    ps[:, bank:bank + 512].rearrange("p (n c) -> p n c", n=8)))
                ev_marks[grp] = s_dv.n
                grp += 1
            QKV_DV = s_dv.n

            if tap_aps:
                nc.gpsimd.wait_ge(s_dv.h, QKV_DV)
                if "q_fm" in tap_aps:
                    for g in range(8):
                        s_dst.inc(nc.gpsimd.dma_start(
                            tap_aps["q_fm"][g * 128:(g + 1) * 128, :],
                            q_fm[:, g * S:(g + 1) * S]), 16)
                if "sg_fm" in tap_aps:
                    for g in range(8):
                        s_dst.inc(nc.gpsimd.dma_start(
                            tap_aps["sg_fm"][g * 128:(g + 1) * 128, :],
                            sg_fm[:, g * S:(g + 1) * S]), 16)
                if "k_fm" in tap_aps:
                    for g in range(4):
                        s_dst.inc(nc.gpsimd.dma_start(
                            tap_aps["k_fm"][g * 128:(g + 1) * 128, :],
                            k_fm[:, g * S:(g + 1) * S]), 16)
                if "v_aug" in tap_aps:
                    for tc in range(TC):
                        s_dst.inc(nc.gpsimd.dma_start(
                            tap_aps["v_aug"][tc * 128:(tc + 1) * 128, :],
                            v_aug[:, tc * 520:(tc + 1) * 520]), 16)

        if upto >= 2:
            ev2 = {}
            grp = 0
            nc.vector.wait_ge(s_ac.h, W_SIG)
            for g in range(2):
                for tc in range(TC):
                    bank = PS_A if grp % 2 == 0 else PS_B
                    dbk = PS_PV + (grp % 2)
                    if grp >= 2:
                        nc.tensor.wait_ge(s_dv.h, ev2[grp - 2])
                    for fi in range(4):
                        ch = 4 * g + fi
                        nc.tensor.matmul(
                            ps[:, bank:bank + 512],
                            sg_fm[:, ch * S + tc * 128:ch * S + (tc + 1) * 128],
                            mem_sb[:, fi * KVW:(fi + 1) * KVW],
                            start=(fi == 0), stop=(fi == 3))
                    for fi in range(4):
                        ch = 4 * g + fi
                        mmi = nc.tensor.matmul(
                            ps[:, dbk:dbk + 1],
                            sg_fm[:, ch * S + tc * 128:ch * S + (tc + 1) * 128],
                            memn_sb[:, fi:fi + 1],
                            start=(fi == 0), stop=(fi == 3))
                    s_mm.inc(mmi)
                    nc.vector.wait_ge(s_mm.h, s_mm.n)
                    s_dv.inc(nc.vector.reciprocal(dtmp[:, grp % 2:grp % 2 + 1],
                                                  ps[:, dbk:dbk + 1]))
                    dve_sync()
                    s_dv.inc(nc.vector.tensor_tensor(
                        dtmp[:, 2 + grp % 2:3 + grp % 2],
                        dtmp[:, grp % 2:grp % 2 + 1], w_col[:, 0:1], ALU.mult))
                    dve_sync()
                    s_dv.inc(nc.vector.tensor_scalar(
                        acc[:, tc * D + g * 512:tc * D + (g + 1) * 512],
                        ps[:, bank:bank + 512],
                        dtmp[:, 2 + grp % 2:3 + grp % 2], None, ALU.mult))
                    ev2[grp] = s_dv.n
                    grp += 1
            AMEM_DV = s_dv.n

            if "attn1" in tap_aps:
                nc.sync.wait_ge(s_dv.h, AMEM_DV)
                for tc in range(TC):
                    s_dst.inc(nc.sync.dma_start(
                        tap_aps["attn1"][tc * 128:(tc + 1) * 128, :],
                        acc[:, tc * D:(tc + 1) * D]), 16)

        if upto >= 3:
            def emit_pv(hh, ii):
                gg = hh * TC + ii
                nc.tensor.wait_ge(s_exp.h, gg + 1)
                if ii == 0:
                    nc.tensor.wait_ge(s_pvev.h, hh)
                n2 = hh // 2
                for k in range(16):
                    off = pv_off(k)
                    mmi = nc.tensor.matmul(
                        ps[:, off:off + 65],
                        pb[ii % 2][:, k * 128:(k + 1) * 128],
                        v_aug[:, ii * 520 + n2 * 65:ii * 520 + (n2 + 1) * 65],
                        start=(ii == 0 and k in (0, 7, 14)),
                        stop=(ii == TC - 1), skip_group_check=True)
                s_pv.inc(mmi)

            nc.scalar.wait_ge(ldw.h, CST_LD)
            for h in range(NQH):
                kh = h // 2
                kb, kc = 64 * (kh % 2), kh // 2
                qc, qb = qloc(h)
                assert qb == kb
                for i in range(TC):
                    g = h * TC + i
                    if g > 0:
                        nc.tensor.wait_ge(s_exp.h, g)
                    for sb in range(4):
                        mmi = nc.tensor.matmul(
                            ps[:, PS_SC + sb * 512:PS_SC + (sb + 1) * 512],
                            k_fm[kb:kb + 64, kc * S + i * 128:kc * S + (i + 1) * 128],
                            q_fm[qb:qb + 64, qc * S + sb * 512:qc * S + (sb + 1) * 512],
                            start=True, stop=True)
                    s_mm.inc(mmi)
                    if i > 0:
                        emit_pv(h, i - 1)
                    elif h > 0:
                        emit_pv(h - 1, TC - 1)
                    nc.scalar.wait_ge(s_mm.h, s_mm.n)
                    if g >= 2:
                        nc.scalar.wait_ge(s_pv.h, g - 1)
                    s_exp.inc(nc.scalar.activation(
                        pb[i % 2][:, :], ps[:, PS_SC:PS_SC + 2048], AF.Exp,
                        bias=cst_sb[:, C_EXPB:C_EXPB + 1], scale=0.125))
            emit_pv(NQH - 1, TC - 1)

            nc.vector.wait_ge(s_dv.h, AMEM_DV)
            for h in range(NQH):
                nc.vector.wait_ge(s_pv.h, (h + 1) * TC)
                dens3 = dens[:, :].rearrange("p (sc hh) -> p sc hh", hh=16)
                for b, nb in ((0, 7), (1, 7), (2, 2)):
                    s_dv.inc(nc.vector.tensor_copy(
                        dens3[:, 7 * b:7 * b + nb, h:h + 1],
                        ps[:, PS_PV + 512 * b:PS_PV + 512 * b + 73 * nb].rearrange(
                            "p (sl c) -> p sl c", c=73)[:, :, 64:65]))
                dve_sync()
                s_dv.inc(nc.vector.tensor_copy(
                    dtmp[:, 0:16].unsqueeze(2), dens3[:, :, h:h + 1]))
                dve_sync()
                s_dv.inc(nc.vector.reciprocal(dtmp[:, 0:16], dtmp[:, 0:16]))
                dve_sync()
                s_dv.inc(nc.vector.tensor_scalar(
                    dtmp[:, 0:16], dtmp[:, 0:16], w_col[:, 1:2], None, ALU.mult))
                dve_sync()
                s_dv.inc(nc.vector.tensor_copy(
                    db[:, :].rearrange("p (sc c) -> p sc c", c=64),
                    dtmp[:, 0:16].unsqueeze(2).broadcast_to([128, 16, 64])))
                dve_sync()
                acc3 = acc[:, :].rearrange("p (sc dd) -> p sc dd", dd=D)
                db3 = db[:, :].rearrange("p (sc c) -> p sc c", c=64)
                for b, nb in ((0, 7), (1, 7), (2, 2)):
                    s_dv.inc(nc.vector.tensor_tensor(
                        dscr[:, 0:nb * 64].rearrange("p (sl c) -> p sl c", c=64),
                        ps[:, PS_PV + 512 * b:PS_PV + 512 * b + 73 * nb].rearrange(
                            "p (sl c) -> p sl c", c=73)[:, :, 0:64],
                        db3[:, 7 * b:7 * b + nb, :], ALU.mult))
                    dve_sync()
                    s_dv.inc(nc.vector.tensor_tensor(
                        acc3[:, 7 * b:7 * b + nb, 64 * h:64 * h + 64],
                        acc3[:, 7 * b:7 * b + nb, 64 * h:64 * h + 64],
                        dscr[:, 0:nb * 64].rearrange("p (sl c) -> p sl c", c=64),
                        ALU.add))
                    dve_sync()
                s_pvev.inc(nc.vector.tensor_copy(dtmp[:, 8:9], w_col[:, 0:1]))
            ATTN_DV = s_dv.n

            if "attn2" in tap_aps:
                nc.sync.wait_ge(s_pvev.h, s_pvev.n)
                for tc in range(TC):
                    s_dst.inc(nc.sync.dma_start(
                        tap_aps["attn2"][tc * 128:(tc + 1) * 128, :],
                        acc[:, tc * D:(tc + 1) * D]), 16)

        def emit_ln_pass1(acc_slice, xin_t, col):
            s_dv.inc(nc.vector.scalar_tensor_tensor(
                acc_slice, xin_t[:, :], 0.0, acc_slice, ALU.add, ALU.add,
                accum_out=stats[:, col:col + 1]))
            dve_sync()
            nc.scalar.wait_ge(s_dv.h, s_dv.n)
            s_ac.inc(nc.scalar.activation(
                sq_scr[:, :], acc_slice, AF.Square,
                accum_out=stats[:, col + 16:col + 17]))

        def emit_ln_stats(base):
            nc.vector.wait_ge(s_ac.h, s_ac.n)
            s_dv.inc(nc.vector.tensor_scalar(
                stats[:, base + 32:base + 48],
                stats[:, base:base + 16], -1.0 / D, None, ALU.mult))
            dve_sync()
            s_dv.inc(nc.vector.tensor_tensor(
                stats[:, base + 48:base + 64],
                stats[:, base + 32:base + 48],
                stats[:, base + 32:base + 48], ALU.mult))
            dve_sync()
            s_dv.inc(nc.vector.scalar_tensor_tensor(
                stats[:, base + 48:base + 64],
                stats[:, base + 16:base + 32], 1.0 / D,
                stats[:, base + 48:base + 64], ALU.mult, ALU.subtract))
            dve_sync()
            nc.scalar.wait_ge(s_dv.h, s_dv.n)
            s_ac.inc(nc.scalar.activation(
                stats[:, base + 48:base + 64],
                stats[:, base + 48:base + 64], AF.Sqrt,
                bias=cst_sb[:, C_EPS:C_EPS + 1]))
            nc.vector.wait_ge(s_ac.h, s_ac.n)
            s_dv.inc(nc.vector.reciprocal(
                stats[:, base + 48:base + 64],
                stats[:, base + 48:base + 64]))
            dve_sync()

        if upto >= 4:
            ln_marks = {}
            nc.sync.wait_ge(s_pv.h, NQH * TC)
            for tc in range(TC):
                xb = xin[tc % 2]
                if tc >= 2:
                    nc.sync.wait_ge(s_dv.h, ln_marks[tc - 2])
                s_xi.inc(nc.sync.dma_start(xb[:, :],
                                           x[tc * 128:(tc + 1) * 128, :]), 16)
                nc.vector.wait_ge(s_xi.h, s_xi.n)
                if tc == 0:
                    nc.vector.wait_ge(s_dv.h, ATTN_DV)
                emit_ln_pass1(acc[:, tc * D:(tc + 1) * D], xb, tc)
                ln_marks[tc] = s_dv.n
            emit_ln_stats(0)
            for tc in range(TC):
                xt = x1t[tc % 2]
                if tc >= 2:
                    nc.vector.wait_ge(s_dx1.h, 16 * (tc - 1))
                    nc.vector.wait_ge(s_dxb.h, 16 * (tc - 1))
                s_dv.inc(nc.vector.tensor_scalar(
                    xt[:, :], acc[:, tc * D:(tc + 1) * D],
                    stats[:, 32 + tc:33 + tc], stats[:, 48 + tc:49 + tc],
                    ALU.add, ALU.mult))
                dve_sync()
                nc.sync.wait_ge(s_dv.h, s_dv.n)
                s_dx1.inc(nc.sync.dma_start(x1d[tc * 128:(tc + 1) * 128, :],
                                            xt[:, :]), 16)
                nc.gpsimd.wait_ge(s_dv.h, s_dv.n)
                s_dxb.inc(nc.gpsimd.dma_start(x1_bf[tc * 128:(tc + 1) * 128, :],
                                              xt[:, :]), 16)
            X1_DONE = s_dx1.n
            X1B_DONE = s_dxb.n

            if "x1" in tap_aps:
                nc.sync.wait_ge(s_dx1.h, X1_DONE)
                s_dst.inc(nc.sync.dma_start(tap_aps["x1"], x1d), 16)

        if upto >= 5:
            x1fm_marks = {}

            def emit_x1fm(s):
                nc.sync.wait_ge(s_dxb.h, X1B_DONE)
                if s >= 2:
                    nc.sync.wait_ge(s_zmm.h, x1fm_marks[s - 2])
                for fc in range(8):
                    s_ft.inc(nc.sync.dma_start(
                        x1fm[s % 2][:, fc * 256:(fc + 1) * 256],
                        x1_bf[s * 256:(s + 1) * 256, fc * 128:(fc + 1) * 128],
                        transpose=True), 16)

            def emit_zgroup(s, cg):
                idx = s * 16 + cg
                nc.tensor.wait_ge(s_ft.h, 128 * (s + 1))
                if idx == 0:
                    nc.tensor.wait_ge(s_po.h, LZ_DONE)
                    nc.tensor.wait_ge(s_pvev.h, NQH)
                if idx > 0:
                    nc.tensor.wait_ge(s_gl.h, idx)
                for l in range(8):
                    ci = cg * 8 + l
                    u = ci % 32
                    gch = u // 4
                    rbase = 64 * ((u % 4) // 2)
                    lzh, lzslot = lzloc(ci)
                    assert lzh == rbase
                    mmi = nc.tensor.matmul(
                        ps[:, PS_Z + l * 256:PS_Z + (l + 1) * 256],
                        lz[lzh:lzh + 64, lzslot * 128:lzslot * 128 + 128],
                        x1fm[s % 2][rbase:rbase + 64, gch * 256:(gch + 1) * 256],
                        start=True, stop=True)
                s_zmm.inc(mmi)
                x1fm_marks[s] = s_zmm.n
                nc.scalar.wait_ge(s_zmm.h, s_zmm.n)
                if idx >= 2:
                    nc.scalar.wait_ge(s_dh.h, 16 * (idx - 1))
                s_gl.inc(nc.scalar.activation(hstage[idx % 2][:, :],
                                              ps[:, PS_Z:PS_Z + 2048], AF.Gelu))
                nc.gpsimd.wait_ge(s_gl.h, s_gl.n)
                s_dh.inc(nc.gpsimd.dma_start(
                    h_dram[cg * 8:(cg + 1) * 8, :, s * 256:(s + 1) * 256]
                    .transpose([1, 0, 2]),
                    hstage[idx % 2][:, :].rearrange("p (l t) -> p l t", l=8)), 16)

            def emit_cproj_block(half, b):
                cp_base = s_cp.n
                nc.sync.wait_ge(ld.h, WC_DONE)
                nc.scalar.wait_ge(s_dh.h, 256 * (2 * b + 2))
                for ci in range(128):
                    cii = ci % 2
                    if cp_base + ci - 1 > 0:
                        nc.sync.wait_ge(s_cp.h, max(0, cp_base + ci - 1))
                        nc.scalar.wait_ge(s_cp.h, max(0, cp_base + ci - 1))
                    s_wt.inc(nc.sync.dma_start(
                        wt_t[cii][:, :],
                        wc_bf[half * 512:(half + 1) * 512,
                              ci * 128:(ci + 1) * 128],
                        transpose=True), 16)
                    s_ht.inc(nc.scalar.dma_start(
                        h_t[cii][:, :],
                        h_dram[ci, :, b * 512:(b + 1) * 512]), 16)
                    nc.tensor.wait_ge(s_wt.h, s_wt.n)
                    nc.tensor.wait_ge(s_ht.h, s_ht.n)
                    if ci == 0:
                        nc.tensor.wait_ge(s_ev.h, s_ev.n)
                    for tjr in range(4):
                        mmi = nc.tensor.matmul(
                            ps[:, PS_CP + tjr * 512:PS_CP + (tjr + 1) * 512],
                            h_t[cii][:, tjr * 128:(tjr + 1) * 128],
                            wt_t[cii][:, :],
                            start=(ci == 0), stop=(ci == 127))
                    s_cp.inc(mmi)
                nc.vector.wait_ge(s_cp.h, s_cp.n)
                for tjr in range(4):
                    tc = 4 * b + tjr
                    s_ev.inc(nc.vector.tensor_copy(
                        acc[:, tc * D + half * 512:tc * D + half * 512 + 512],
                        ps[:, PS_CP + tjr * 512:PS_CP + (tjr + 1) * 512]))

            emit_x1fm(0)
            for cg in range(16):
                emit_zgroup(0, cg)
            emit_x1fm(1)
            for cg in range(16):
                emit_zgroup(1, cg)
            for b in range(4):
                emit_cproj_block(0, b)
                if 2 * b + 2 < 8:
                    for soff in (2, 3):
                        emit_x1fm(2 * b + soff)
                        for cg in range(16):
                            emit_zgroup(2 * b + soff, cg)
            for b in range(4):
                emit_cproj_block(1, b)

            if "hh" in tap_aps:
                nc.gpsimd.wait_ge(s_dh.h, s_dh.n)
                for ci in range(0, CC, 16):
                    s_dst.inc(nc.gpsimd.dma_start(
                        tap_aps["hh"][ci:ci + 16], h_dram[ci:ci + 16]), 16)

            ln2_marks = {}
            for tc in range(TC):
                xb = xin[tc % 2]
                if tc >= 2:
                    nc.sync.wait_ge(s_dv.h, ln2_marks[tc - 2])
                nc.sync.wait_ge(s_ev.h, s_ev.n)
                s_xi.inc(nc.sync.dma_start(xb[:, :],
                                           x1d[tc * 128:(tc + 1) * 128, :]), 16)
                nc.vector.wait_ge(s_xi.h, s_xi.n)
                nc.vector.wait_ge(s_ev.h, s_ev.n)
                emit_ln_pass1(acc[:, tc * D:(tc + 1) * D], xb, 64 + tc)
                ln2_marks[tc] = s_dv.n
            emit_ln_stats(64)
            for tc in range(TC):
                ot = outt[tc % 2]
                if tc >= 2:
                    nc.vector.wait_ge(s_ou.h, 16 * (tc - 1))
                s_dv.inc(nc.vector.tensor_scalar(
                    ot[:, :], acc[:, tc * D:(tc + 1) * D],
                    stats[:, 96 + tc:97 + tc], stats[:, 112 + tc:113 + tc],
                    ALU.add, ALU.mult))
                dve_sync()
                nc.sync.wait_ge(s_dv.h, s_dv.n)
                s_ou.inc(nc.sync.dma_start(out[tc * 128:(tc + 1) * 128, :],
                                           ot[:, :]), 16)

        if s_ou.n > 0:
            nc.sync.wait_ge(s_ou.h, s_ou.n)
        if s_dst.n > 0:
            nc.sync.wait_ge(s_dst.h, s_dst.n)
            nc.gpsimd.wait_ge(s_dst.h, s_dst.n)

    return nc


def _consts():
    cst = np.zeros((128, 8), dtype=np.float32)
    cst[:, C_EXPB] = -4.0
    cst[:, C_EPS] = 1e-5
    return cst


def _ident():
    return np.eye(128, dtype=np.float32)


def _in_map(inputs, b):
    a = {k: np.asarray(v) for k, v in inputs.items()}
    return {
        "x": np.ascontiguousarray(a["L0_attn_x"][b]),
        "wq": a["L0_attn_Wq"], "wk": a["L0_attn_Wk"], "wv": a["L0_attn_Wv"],
        "mem": a["L0_attn_memory"],
        "memn": a["L0_attn_memory_norm"],
        "memw": a["L0_attn_memory_weight"],
        "rm": np.ascontiguousarray(a["L0_ffn_rail_maker"].reshape(4 * 64 * 16, 16)),
        "rup": np.ascontiguousarray(a["L0_ffn_rail_up_proj"].reshape(4 * 64 * 64, 16)),
        "wc": a["L0_ffn_w_cproj"],
        "cst": _consts(),
        "idin": _ident(),
    }


def kernel(**inputs):
    nc = build(upto=99)
    in_maps = [_in_map(inputs, b) for b in range(8)]
    res = run_bass_kernel_spmd(nc, in_maps, core_ids=list(range(8)))
    return np.stack([np.asarray(r["out"]) for r in res.results], axis=0)


def kernel_dev(inputs, upto=99, taps=(), cores=(0,), dbg_dram=False):
    nc = build(upto=upto, taps=taps, dbg_dram=dbg_dram)
    in_maps = [_in_map(inputs, b) for b in cores]
    res = run_bass_kernel_spmd(nc, in_maps, core_ids=list(range(len(cores))))
    return res.results


# version 05
# speedup vs baseline: 2.0689x; 2.0689x over previous
import numpy as np
from contextlib import ExitStack

import concourse.bass as bass
import concourse.mybir as mybir
from concourse.bass_utils import run_bass_kernel_spmd

F32 = mybir.dt.float32
BF16 = mybir.dt.bfloat16
AF = mybir.ActivationFunctionType
ALU = mybir.AluOpType

S = 2048
D = 1024
KVW = 512
NQH = 16
TC = 16
CC = 128

C_EXPB = 0
C_EPS = 1


class Sem:
    def __init__(self, st, nc, name):
        self.h = st.enter_context(nc.semaphore(name))
        self.n = 0

    def inc(self, instr, k=1):
        instr.then_inc(self.h, k)
        self.n += k
        return self.n


def lz_loc(h, r):
    p, rr = r // 2, r % 2
    q = p % 4
    band = q // 2
    sib = h * 16 + (p // 4) * 2 + (q % 2)
    row0 = 64 * band + 32 * (q % 2) + 16 * rr
    col0 = sib * 128 + 64 * rr
    return row0, col0


def lz_blk(h, p):
    q = p % 4
    band = q // 2
    sib = h * 16 + (p // 4) * 2 + (q % 2)
    return 64 * band, sib * 128


def build(upto=99, taps=(), dbg_dram=False):
    nc = bass.Bass("TRN2", target_bir_lowering=False)
    taps = set(taps)
    SCR = "ExternalOutput" if dbg_dram else "Internal"

    x = nc.dram_tensor("x", [S, D], F32, kind="ExternalInput").ap()
    wq = nc.dram_tensor("wq", [D, D], F32, kind="ExternalInput").ap()
    wk = nc.dram_tensor("wk", [D, KVW], F32, kind="ExternalInput").ap()
    wv = nc.dram_tensor("wv", [D, KVW], F32, kind="ExternalInput").ap()
    mem = nc.dram_tensor("mem", [KVW, KVW], F32, kind="ExternalInput").ap()
    memn = nc.dram_tensor("memn", [KVW], F32, kind="ExternalInput").ap()
    memw = nc.dram_tensor("memw", [1], F32, kind="ExternalInput").ap()
    rm = nc.dram_tensor("rm", [4096, 16], F32, kind="ExternalInput").ap()
    rup = nc.dram_tensor("rup", [16384, 16], F32, kind="ExternalInput").ap()
    wc = nc.dram_tensor("wc", [D, 16384], F32, kind="ExternalInput").ap()
    cst = nc.dram_tensor("cst", [128, 8], F32, kind="ExternalInput").ap()
    out = nc.dram_tensor("out", [S, D], F32, kind="ExternalOutput").ap()

    x_bf = nc.dram_tensor("x_bf", [S, D], BF16, kind=SCR).ap()
    wc_bf = nc.dram_tensor("wc_bf", [D, 16384], BF16, kind=SCR).ap()
    x1d = nc.dram_tensor("x1d", [S, D], F32, kind=SCR).ap()
    x1_bf = nc.dram_tensor("x1_bf", [S, D], BF16, kind=SCR).ap()
    wt_dram = nc.dram_tensor("wt_dram", [2, CC, 128, 512], BF16, kind=SCR).ap()
    rm_pad = nc.dram_tensor("rm_pad", [4096, 128], BF16, kind=SCR).ap()
    rup_pad = nc.dram_tensor("rup_pad", [16384, 128], BF16, kind=SCR).ap()
    c_dram = nc.dram_tensor("c_dram", [128, 8192], BF16, kind=SCR).ap()

    tap_aps = {}
    for name, shp in [("q_fm", [D, S]), ("k_fm", [KVW, S]), ("sg_fm", [D, S]),
                      ("v_aug", [S, 8 * 65]),
                      ("attn1", [S, D]), ("attn2", [S, D]), ("x1", [S, D]),
                      ("lz", [128, 8192])]:
        if name in taps:
            tap_aps[name] = nc.dram_tensor(
                "tap_" + name, shp, F32, kind="ExternalOutput").ap()

    with ExitStack() as st:
        E = st.enter_context

        cst_sb = E(nc.sbuf_tensor("cst_sb", [128, 8], F32))
        A = E(nc.sbuf_tensor("A", [128, TC * D], F32))
        acc = A
        x_fm = A[:, 0:8192].bitcast(BF16)
        wq_sb = A[:, 8192:12288].bitcast(BF16)
        wk_sb = A[:, 12288:14336].bitcast(BF16)
        wv_sb = A[:, 14336:16384].bitcast(BF16)
        A_bf = A[:, :].bitcast(BF16)
        mem_sb = E(nc.sbuf_tensor("mem_sb", [128, 4 * KVW], BF16))
        memn_sb = E(nc.sbuf_tensor("memn_sb", [128, 4], BF16))
        B = E(nc.sbuf_tensor("B", [128, 8 * S], BF16))
        q_fm = B
        ut_all = B
        Cb = E(nc.sbuf_tensor("Cb", [128, 8 * S], BF16))
        sg_fm = Cb
        Db = E(nc.sbuf_tensor("Db", [128, 4 * S], BF16))
        k_fm = Db
        mt_all = Db
        x1fm = [Db[:, 0:4096], Db[:, 4096:8192]]
        Eb = E(nc.sbuf_tensor("Eb", [128, TC * 520], BF16))
        v_aug = Eb
        Ef = Eb[:, 0:8192].bitcast(F32)
        x1t = [Ef[:, 0:1024], Ef[:, 1024:2048]]
        stage = Ef[:, 0:2048]
        sq_scr = Ef[:, 2048:3072]
        pb = [E(nc.sbuf_tensor(f"pb{i}", [128, S], BF16)) for i in range(2)]
        xin = [pb[0][:, :].bitcast(F32), pb[1][:, :].bitcast(F32)]
        G = E(nc.sbuf_tensor("G", [128, 4096], F32))
        wt_ring = G[:, 0:1024].bitcast(BF16)
        ot = [G[:, 1024:2048], G[:, 1024:2048]]
        outt = [G[:, 2048:3072], G[:, 3072:4096]]
        w_col = E(nc.sbuf_tensor("w_col", [128, 4], F32))
        tmp_a = G[:, 0:512]
        tmp_b = G[:, 512:1024]
        tmp_c = G[:, 1024:1536]
        dtmp = E(nc.sbuf_tensor("dtmp", [128, 16], F32))
        db = Cb[:, 0:2048].bitcast(F32)
        dscr = Cb[:, 2048:3072].bitcast(F32)
        cstage = Cb[:, 2048:3072]
        dens = Cb[:, 3072:3584].bitcast(F32)
        stats = E(nc.sbuf_tensor("stats", [128, 72], F32))
        wst = [E(nc.sbuf_tensor(f"wst{i}", [128, 512], BF16)) for i in range(2)]
        lz = E(nc.sbuf_tensor("lz", [128, 8192], BF16))

        ps = E(nc.psum_tensor("ps", [128, 4096], F32))
        PS_SC = 0
        PS_PV = 2048
        PS_B = 3072
        PS_A = 3584

        def pv_off(k):
            return PS_PV + 512 * (k // 7) + 73 * (k % 7)

        def qloc(n):
            j = n // 2
            if j % 2 == 0:
                return j + (n % 2), 0
            return j - 1 + (n % 2), 64

        def h_ap(fc, ntiles=1):
            if fc < 64:
                ar, base = A_bf, fc
            elif fc < 96:
                ar, base = B, fc - 64
            else:
                ar, base = Cb, fc - 96
            return ar[:, base * 512:(base + ntiles) * 512]

        names = ["ld", "ldw", "ldx", "mm", "exp", "pv", "pvev", "dv", "ac",
                 "dst", "dx1", "dxb", "gl", "zmm", "cp", "ev", "ou", "po",
                 "tp", "xi", "ft", "wt", "wg", "ws", "lzl", "l2"]
        sm = {n: Sem(st, nc, n) for n in names}
        ld = sm["ld"]; ldw = sm["ldw"]; ldx = sm["ldx"]
        s_mm = sm["mm"]; s_exp = sm["exp"]; s_pv = sm["pv"]; s_pvev = sm["pvev"]
        s_dv = sm["dv"]; s_ac = sm["ac"]; s_dst = sm["dst"]
        s_dx1 = sm["dx1"]; s_dxb = sm["dxb"]; s_gl = sm["gl"]; s_zmm = sm["zmm"]
        s_cp = sm["cp"]; s_ev = sm["ev"]; s_ou = sm["ou"]; s_po = sm["po"]
        s_tp = sm["tp"]; s_xi = sm["xi"]; s_ft = sm["ft"]
        s_wt = sm["wt"]; s_wg = sm["wg"]; s_ws = sm["ws"]; s_lz = sm["lzl"]
        s_l2 = sm["l2"]

        def dve_sync():
            nc.vector.wait_ge(s_dv.h, s_dv.n)

        for tb in range(4):
            for i in range(4):
                r0 = (tb * 4 + i) * 128
                ldx.inc(nc.gpsimd.dma_start(x_bf[r0:r0 + 128, :],
                                            x[r0:r0 + 128, :]), 16)
        ldw.inc(nc.gpsimd.dma_start(cst_sb[:, :], cst), 16)
        CST_LD = ldw.n
        ldw.inc(nc.gpsimd.dma_start(
            wq_sb[:, :].rearrange("p (g f) -> p g f", g=8),
            wq.rearrange("(g p) f -> p g f", p=128)), 16)
        ldw.inc(nc.gpsimd.dma_start(
            wk_sb[:, :].rearrange("p (g f) -> p g f", g=8),
            wk.rearrange("(g p) f -> p g f", p=128)), 16)
        ldw.inc(nc.gpsimd.dma_start(
            wv_sb[:, :].rearrange("p (g f) -> p g f", g=8),
            wv.rearrange("(g p) f -> p g f", p=128)), 16)
        ldw.inc(nc.gpsimd.dma_start(
            mem_sb[:, :].rearrange("p (g f) -> p g f", g=4),
            mem.rearrange("(g p) f -> p g f", p=128)), 16)
        ldw.inc(nc.gpsimd.dma_start(
            memn_sb[:, :], memn.rearrange("(g p) -> p g", p=128),
            allow_slow_non_contiguous=True), 16)
        ldw.inc(nc.gpsimd.dma_start(rm_pad[:, 0:16], rm), 16)
        ldw.inc(nc.gpsimd.dma_start(rup_pad[0:8192, 0:16], rup[0:8192, :]), 16)
        ldw.inc(nc.gpsimd.dma_start(rup_pad[8192:16384, 0:16],
                                    rup[8192:16384, :]), 16)
        W_SMALL = ldw.n
        ldw.inc(nc.gpsimd.dma_start(
            w_col[:, 2:3], memw[0:1].unsqueeze(0).broadcast_to([128, 1])), 16)
        MEMW_LD = ldw.n
        WC_HALF = [4 * 16, 8 * 16]

        nc.scalar.wait_ge(ldw.h, MEMW_LD)
        s_ac.inc(nc.scalar.activation(w_col[:, 0:1], w_col[:, 2:3], AF.Sigmoid))
        W_SIG = s_ac.n
        nc.vector.wait_ge(s_ac.h, W_SIG)
        s_dv.inc(nc.vector.tensor_scalar(w_col[:, 1:2], w_col[:, 0:1], -1.0, 1.0,
                                         ALU.mult, ALU.add))
        s_dv.inc(nc.vector.memset(v_aug[:, :], 1.0))
        s_dv.inc(nc.vector.memset(lz[:, :], 0.0))
        MEMSET_DONE = s_dv.n

        nc.sync.wait_ge(ldw.h, W_SMALL)
        for c in range(8):
            s_tp.inc(nc.sync.dma_start(
                mt_all[:, c * 512:(c + 1) * 512],
                rm_pad[c * 512:(c + 1) * 512, :], transpose=True), 16)
        for c in range(32):
            s_tp.inc(nc.sync.dma_start(
                ut_all[:, c * 512:(c + 1) * 512],
                rup_pad[c * 512:(c + 1) * 512, :], transpose=True), 16)
        MTUT_DONE = s_tp.n
        for tb in range(4):
            nc.sync.wait_ge(ldx.h, 16 * 4 * (tb + 1))
            for fc in range(8):
                s_tp.inc(nc.sync.dma_start(
                    x_fm[:, fc * S + tb * 512:fc * S + (tb + 1) * 512],
                    x_bf[tb * 512:(tb + 1) * 512, fc * 128:(fc + 1) * 128],
                    transpose=True), 16)
        XFM_DONE = s_tp.n
        XFM_TB = [MTUT_DONE + 8 * 16 * (tb + 1) for tb in range(4)]

        nc.tensor.wait_ge(s_tp.h, MTUT_DONE)
        cb_marks = {}
        for b in range(32):
            bank = PS_A if b % 2 == 0 else PS_B
            if b >= 2:
                nc.tensor.wait_ge(s_dv.h, cb_marks[b - 2])
            for e in range(8):
                t = b * 8 + e
                mmi = nc.tensor.matmul(
                    ps[0:16, bank + e * 64:bank + (e + 1) * 64],
                    mt_all[0:16, t * 16:(t + 1) * 16],
                    ut_all[0:16, t * 64:(t + 1) * 64],
                    start=True, stop=True, skip_group_check=True)
            s_mm.inc(mmi)
            nc.vector.wait_ge(s_mm.h, s_mm.n)
            if b >= 2:
                nc.vector.wait_ge(s_po.h, 16 * 8 * (b - 1))
            s_dv.inc(nc.vector.tensor_copy(
                cstage[0:16, (b % 2) * 512:(b % 2 + 1) * 512],
                ps[0:16, bank:bank + 512]))
            cb_marks[b] = s_dv.n
            nc.gpsimd.wait_ge(s_dv.h, s_dv.n)
            for e in range(8):
                t = b * 8 + e
                h_, r_ = t // 64, t % 64
                row0, col0 = lz_loc(h_, r_)
                s_po.inc(nc.gpsimd.dma_start(
                    c_dram[row0:row0 + 16, col0:col0 + 64],
                    cstage[0:16, (b % 2) * 512 + e * 64:
                           (b % 2) * 512 + (e + 1) * 64]), 16)
        C_WRITES = s_po.n
        C_EVACS = s_dv.n
        for j in range(8):
            ld.inc(nc.gpsimd.dma_start(wc_bf[128 * j:128 * (j + 1), :],
                                       wc[128 * j:128 * (j + 1), :]), 16)
        nc.gpsimd.wait_ge(s_po.h, C_WRITES)
        nc.gpsimd.wait_ge(s_dv.h, MEMSET_DONE)
        for h_ in range(4):
            for r_ in range(64):
                row0, col0 = lz_loc(h_, r_)
                s_lz.inc(nc.gpsimd.dma_start(
                    lz[row0:row0 + 16, col0:col0 + 64],
                    c_dram[row0:row0 + 16, col0:col0 + 64]), 16)
        LZ_DONE = s_lz.n

        if "lz" in tap_aps:
            nc.gpsimd.wait_ge(s_lz.h, LZ_DONE)
            s_dst.inc(nc.gpsimd.dma_start(tap_aps["lz"], lz[:, :]), 16)

        pairs = [(half, ci) for half in range(2) for ci in range(CC)]
        tp_marks = {}
        st_marks = {}
        for nst, (half, ci) in enumerate(pairs):
            sl3 = nst % 2
            nc.sync.wait_ge(ld.h, WC_HALF[half])
            if nst >= 2:
                nc.sync.wait_ge(s_ws.h, st_marks[nst - 2])
            s_ws.inc(nc.sync.dma_start(
                wst[sl3][:, :],
                wc_bf[half * 512:(half + 1) * 512,
                      ci * 128:(ci + 1) * 128],
                transpose=True), 16)
            tp_marks[nst] = s_ws.n
            if nst >= 1:
                ph, pc = pairs[nst - 1]
                nc.sync.wait_ge(s_ws.h, tp_marks[nst - 1])
                s_ws.inc(nc.sync.dma_start(
                    wt_dram[ph, pc], wst[(nst - 1) % 2][:, :]), 16)
                st_marks[nst - 1] = s_ws.n
        nst = len(pairs)
        ph, pc = pairs[nst - 1]
        nc.sync.wait_ge(s_ws.h, tp_marks[nst - 1])
        s_ws.inc(nc.sync.dma_start(
            wt_dram[ph, pc], wst[(nst - 1) % 2][:, :]), 16)
        st_marks[nst - 1] = s_ws.n
        WS_DONE = s_ws.n

        def ws_need(half, ci):
            return st_marks[half * CC + ci]

        if upto >= 1:
            ev_marks = {}
            grp = 0
            nc.tensor.wait_ge(ldw.h, W_SMALL)
            nc.tensor.wait_ge(s_dv.h, C_EVACS)
            nc.vector.wait_ge(s_po.h, C_WRITES)
            for tb in range(4):
                nc.tensor.wait_ge(s_tp.h, XFM_TB[tb])
                for j in range(12):
                    bank = PS_A if grp % 2 == 0 else PS_B
                    if grp >= 2:
                        nc.tensor.wait_ge(s_dv.h, ev_marks[grp - 2])
                    for fi in range(8):
                        if j < 8:
                            lhsT = wq_sb[:, fi * D + j * 128:fi * D + (j + 1) * 128]
                        else:
                            jj = j - 8
                            lhsT = wk_sb[:, fi * KVW + jj * 128:
                                         fi * KVW + (jj + 1) * 128]
                        mmi = nc.tensor.matmul(
                            ps[:, bank:bank + 512], lhsT,
                            x_fm[:, fi * S + tb * 512:fi * S + (tb + 1) * 512],
                            start=(fi == 0), stop=(fi == 7))
                    s_mm.inc(mmi)
                    nc.vector.wait_ge(s_mm.h, s_mm.n)
                    if j < 8:
                        sl = slice(j * S + tb * 512, j * S + (tb + 1) * 512)
                        for hidx in range(2):
                            qc_, qb_ = qloc(2 * j + hidx)
                            s_dv.inc(nc.vector.tensor_copy(
                                q_fm[qb_:qb_ + 64,
                                     qc_ * S + tb * 512:qc_ * S + (tb + 1) * 512],
                                ps[64 * hidx:64 * hidx + 64, bank:bank + 512]))
                        s_dv.inc(nc.vector.tensor_scalar(
                            tmp_a[:, :], ps[:, bank:bank + 512], 0.0, None, ALU.min))
                        s_dv.inc(nc.vector.tensor_scalar(
                            tmp_b[:, :], ps[:, bank:bank + 512], 0.0, None, ALU.max))
                        nc.scalar.wait_ge(s_dv.h, s_dv.n)
                        s_ac.inc(nc.scalar.activation(tmp_c[:, :], tmp_a[:, :],
                                                      AF.Exp))
                        nc.vector.wait_ge(s_ac.h, s_ac.n)
                        s_dv.inc(nc.vector.scalar_tensor_tensor(
                            sg_fm[:, sl], tmp_c[:, :], -1.0, tmp_b[:, :],
                            ALU.add, ALU.add))
                        dve_sync()
                    else:
                        jj = j - 8
                        sl = slice(jj * S + tb * 512, jj * S + (tb + 1) * 512)
                        s_dv.inc(nc.vector.tensor_copy(k_fm[:, sl],
                                                       ps[:, bank:bank + 512]))
                    ev_marks[grp] = s_dv.n
                    grp += 1
                for tc in range(4 * tb, 4 * tb + 4):
                    bank = PS_A if grp % 2 == 0 else PS_B
                    if grp >= 2:
                        nc.tensor.wait_ge(s_dv.h, ev_marks[grp - 2])
                    for fi in range(8):
                        mmi = nc.tensor.matmul(
                            ps[:, bank:bank + 512],
                            x_fm[:, fi * S + tc * 128:fi * S + (tc + 1) * 128],
                            wv_sb[:, fi * KVW:(fi + 1) * KVW],
                            start=(fi == 0), stop=(fi == 7))
                    s_mm.inc(mmi)
                    nc.vector.wait_ge(s_mm.h, s_mm.n)
                    s_dv.inc(nc.vector.tensor_copy(
                        v_aug[:, tc * 520:(tc + 1) * 520].rearrange(
                            "p (n c) -> p n c", n=8)[:, :, 0:64],
                        ps[:, bank:bank + 512].rearrange("p (n c) -> p n c", n=8)))
                    ev_marks[grp] = s_dv.n
                    grp += 1
            QKV_DV = s_dv.n

            if tap_aps:
                nc.gpsimd.wait_ge(s_dv.h, QKV_DV)
                if "q_fm" in tap_aps:
                    for g in range(8):
                        s_dst.inc(nc.gpsimd.dma_start(
                            tap_aps["q_fm"][g * 128:(g + 1) * 128, :],
                            q_fm[:, g * S:(g + 1) * S]), 16)
                if "sg_fm" in tap_aps:
                    for g in range(8):
                        s_dst.inc(nc.gpsimd.dma_start(
                            tap_aps["sg_fm"][g * 128:(g + 1) * 128, :],
                            sg_fm[:, g * S:(g + 1) * S]), 16)
                if "k_fm" in tap_aps:
                    for g in range(4):
                        s_dst.inc(nc.gpsimd.dma_start(
                            tap_aps["k_fm"][g * 128:(g + 1) * 128, :],
                            k_fm[:, g * S:(g + 1) * S]), 16)
                if "v_aug" in tap_aps:
                    for tc in range(TC):
                        s_dst.inc(nc.gpsimd.dma_start(
                            tap_aps["v_aug"][tc * 128:(tc + 1) * 128, :],
                            v_aug[:, tc * 520:(tc + 1) * 520]), 16)

        if upto >= 2:
            ev2 = {}
            grp = 0
            nc.vector.wait_ge(s_ac.h, W_SIG)
            for g in range(2):
                for tc in range(TC):
                    bank = PS_A if grp % 2 == 0 else PS_B
                    dbk = PS_PV + (grp % 2)
                    if grp >= 2:
                        nc.tensor.wait_ge(s_dv.h, ev2[grp - 2])
                    for fi in range(4):
                        ch = 4 * g + fi
                        nc.tensor.matmul(
                            ps[:, bank:bank + 512],
                            sg_fm[:, ch * S + tc * 128:ch * S + (tc + 1) * 128],
                            mem_sb[:, fi * KVW:(fi + 1) * KVW],
                            start=(fi == 0), stop=(fi == 3))
                    for fi in range(4):
                        ch = 4 * g + fi
                        mmi = nc.tensor.matmul(
                            ps[:, dbk:dbk + 1],
                            sg_fm[:, ch * S + tc * 128:ch * S + (tc + 1) * 128],
                            memn_sb[:, fi:fi + 1],
                            start=(fi == 0), stop=(fi == 3))
                    s_mm.inc(mmi)
                    nc.vector.wait_ge(s_mm.h, s_mm.n)
                    s_dv.inc(nc.vector.reciprocal(dtmp[:, grp % 2:grp % 2 + 1],
                                                  ps[:, dbk:dbk + 1]))
                    dve_sync()
                    s_dv.inc(nc.vector.tensor_tensor(
                        dtmp[:, 2 + grp % 2:3 + grp % 2],
                        dtmp[:, grp % 2:grp % 2 + 1], w_col[:, 0:1], ALU.mult))
                    dve_sync()
                    s_dv.inc(nc.vector.tensor_scalar(
                        acc[:, tc * D + g * 512:tc * D + (g + 1) * 512],
                        ps[:, bank:bank + 512],
                        dtmp[:, 2 + grp % 2:3 + grp % 2], None, ALU.mult))
                    ev2[grp] = s_dv.n
                    grp += 1
            AMEM_DV = s_dv.n

            if "attn1" in tap_aps:
                nc.sync.wait_ge(s_dv.h, AMEM_DV)
                for tc in range(TC):
                    s_dst.inc(nc.sync.dma_start(
                        tap_aps["attn1"][tc * 128:(tc + 1) * 128, :],
                        acc[:, tc * D:(tc + 1) * D]), 16)

        if upto >= 3:
            def emit_pv(hh, ii):
                gg = hh * TC + ii
                nc.tensor.wait_ge(s_exp.h, 2 * gg + 2)
                if ii == 0:
                    nc.tensor.wait_ge(s_pvev.h, hh)
                n2 = hh // 2
                for k in range(16):
                    off = pv_off(k)
                    mmi = nc.tensor.matmul(
                        ps[:, off:off + 65],
                        pb[ii % 2][:, k * 128:(k + 1) * 128],
                        v_aug[:, ii * 520 + n2 * 65:ii * 520 + (n2 + 1) * 65],
                        start=(ii == 0 and k in (0, 7, 14)),
                        stop=(ii == TC - 1), skip_group_check=True)
                s_pv.inc(mmi)

            nc.scalar.wait_ge(ldw.h, CST_LD)
            for h in range(NQH):
                kh = h // 2
                kb, kc = 64 * (kh % 2), kh // 2
                qc, qb = qloc(h)
                assert qb == kb
                for i in range(TC):
                    g = h * TC + i
                    for sbh in range(2):
                        u = 2 * g + sbh
                        reg = PS_SC + (u % 2) * 1024
                        if u > 1:
                            nc.tensor.wait_ge(s_exp.h, u - 1)
                        for sb in range(2):
                            so = 1024 * sbh + 512 * sb
                            mmi = nc.tensor.matmul(
                                ps[:, reg + sb * 512:reg + (sb + 1) * 512],
                                k_fm[kb:kb + 64,
                                     kc * S + i * 128:kc * S + (i + 1) * 128],
                                q_fm[qb:qb + 64,
                                     qc * S + so:qc * S + so + 512],
                                start=True, stop=True)
                        s_mm.inc(mmi)
                        if sbh == 1:
                            if i > 0:
                                emit_pv(h, i - 1)
                            elif h > 0:
                                emit_pv(h - 1, TC - 1)
                        nc.scalar.wait_ge(s_mm.h, s_mm.n)
                        if g >= 2:
                            nc.scalar.wait_ge(s_pv.h, g - 1)
                        s_exp.inc(nc.scalar.activation(
                            pb[i % 2][:, 1024 * sbh:1024 * (sbh + 1)],
                            ps[:, reg:reg + 1024], AF.Exp,
                            bias=cst_sb[:, C_EXPB:C_EXPB + 1], scale=0.125))
            emit_pv(NQH - 1, TC - 1)
            EXP_ALL = s_exp.n

            nc.vector.wait_ge(s_dv.h, AMEM_DV)
            for h in range(NQH):
                nc.vector.wait_ge(s_pv.h, (h + 1) * TC)
                dens3 = dens[:, :].rearrange("p (sc hh) -> p sc hh", hh=16)
                for b, nb in ((0, 7), (1, 7), (2, 2)):
                    s_dv.inc(nc.vector.tensor_copy(
                        dens3[:, 7 * b:7 * b + nb, h:h + 1],
                        ps[:, PS_PV + 512 * b:PS_PV + 512 * b + 73 * nb].rearrange(
                            "p (sl c) -> p sl c", c=73)[:, :, 64:65]))
                dve_sync()
                s_dv.inc(nc.vector.tensor_copy(
                    dtmp[:, 0:16].unsqueeze(2), dens3[:, :, h:h + 1]))
                dve_sync()
                s_dv.inc(nc.vector.reciprocal(dtmp[:, 0:16], dtmp[:, 0:16]))
                dve_sync()
                s_dv.inc(nc.vector.tensor_scalar(
                    dtmp[:, 0:16], dtmp[:, 0:16], w_col[:, 1:2], None, ALU.mult))
                dve_sync()
                s_dv.inc(nc.vector.tensor_copy(
                    db[:, :].rearrange("p (sc c) -> p sc c", c=64),
                    dtmp[:, 0:16].unsqueeze(2).broadcast_to([128, 16, 64])))
                dve_sync()
                acc3 = acc[:, :].rearrange("p (sc dd) -> p sc dd", dd=D)
                db3 = db[:, :].rearrange("p (sc c) -> p sc c", c=64)
                for b, nb in ((0, 7), (1, 7), (2, 2)):
                    s_dv.inc(nc.vector.tensor_tensor(
                        dscr[:, 0:nb * 64].rearrange("p (sl c) -> p sl c", c=64),
                        ps[:, PS_PV + 512 * b:PS_PV + 512 * b + 73 * nb].rearrange(
                            "p (sl c) -> p sl c", c=73)[:, :, 0:64],
                        db3[:, 7 * b:7 * b + nb, :], ALU.mult))
                    dve_sync()
                    s_dv.inc(nc.vector.tensor_tensor(
                        acc3[:, 7 * b:7 * b + nb, 64 * h:64 * h + 64],
                        acc3[:, 7 * b:7 * b + nb, 64 * h:64 * h + 64],
                        dscr[:, 0:nb * 64].rearrange("p (sl c) -> p sl c", c=64),
                        ALU.add))
                    dve_sync()
                s_pvev.inc(nc.vector.tensor_copy(dtmp[:, 8:9], w_col[:, 0:1]))
            ATTN_DV = s_dv.n

            if "attn2" in tap_aps:
                nc.sync.wait_ge(s_pvev.h, s_pvev.n)
                for tc in range(TC):
                    s_dst.inc(nc.sync.dma_start(
                        tap_aps["attn2"][tc * 128:(tc + 1) * 128, :],
                        acc[:, tc * D:(tc + 1) * D]), 16)

        def emit_ln_pass1(acc_slice, xin_t, col):
            s_dv.inc(nc.vector.scalar_tensor_tensor(
                acc_slice, xin_t[:, :], 0.0, acc_slice, ALU.add, ALU.add,
                accum_out=stats[:, col:col + 1]))
            dve_sync()
            nc.scalar.wait_ge(s_dv.h, s_dv.n)
            s_ac.inc(nc.scalar.activation(
                sq_scr[:, :], acc_slice, AF.Square,
                accum_out=stats[:, col + 16:col + 17]))

        def emit_ln_stats(base):
            nc.vector.wait_ge(s_ac.h, s_ac.n)
            s_dv.inc(nc.vector.tensor_scalar(
                stats[:, base + 32:base + 48],
                stats[:, base:base + 16], -1.0 / D, None, ALU.mult))
            dve_sync()
            s_dv.inc(nc.vector.tensor_tensor(
                stats[:, base + 48:base + 64],
                stats[:, base + 32:base + 48],
                stats[:, base + 32:base + 48], ALU.mult))
            dve_sync()
            s_dv.inc(nc.vector.scalar_tensor_tensor(
                stats[:, base + 48:base + 64],
                stats[:, base + 16:base + 32], 1.0 / D,
                stats[:, base + 48:base + 64], ALU.mult, ALU.subtract))
            dve_sync()
            nc.scalar.wait_ge(s_dv.h, s_dv.n)
            s_ac.inc(nc.scalar.activation(
                stats[:, base + 48:base + 64],
                stats[:, base + 48:base + 64], AF.Sqrt,
                bias=cst_sb[:, C_EPS:C_EPS + 1]))
            nc.vector.wait_ge(s_ac.h, s_ac.n)
            s_dv.inc(nc.vector.reciprocal(
                stats[:, base + 48:base + 64],
                stats[:, base + 48:base + 64]))
            dve_sync()

        if upto >= 4:
            ln_marks = {}
            nc.sync.wait_ge(s_pv.h, NQH * TC)
            for tc in range(TC):
                xb = xin[tc % 2]
                if tc >= 2:
                    nc.sync.wait_ge(s_dv.h, ln_marks[tc - 2])
                s_xi.inc(nc.sync.dma_start(xb[:, :],
                                           x[tc * 128:(tc + 1) * 128, :]), 16)
                nc.vector.wait_ge(s_xi.h, s_xi.n)
                if tc == 0:
                    nc.vector.wait_ge(s_dv.h, ATTN_DV)
                emit_ln_pass1(acc[:, tc * D:(tc + 1) * D], xb, tc)
                ln_marks[tc] = s_dv.n
            emit_ln_stats(0)
            for tc in range(TC):
                xt = x1t[tc % 2]
                if tc >= 2:
                    nc.vector.wait_ge(s_dx1.h, 16 * (tc - 1))
                    nc.vector.wait_ge(s_dxb.h, 16 * (tc - 1))
                s_dv.inc(nc.vector.tensor_scalar(
                    xt[:, :], acc[:, tc * D:(tc + 1) * D],
                    stats[:, 32 + tc:33 + tc], stats[:, 48 + tc:49 + tc],
                    ALU.add, ALU.mult))
                dve_sync()
                nc.sync.wait_ge(s_dv.h, s_dv.n)
                s_dx1.inc(nc.sync.dma_start(x1d[tc * 128:(tc + 1) * 128, :],
                                            xt[:, :]), 16)
                nc.gpsimd.wait_ge(s_dv.h, s_dv.n)
                s_dxb.inc(nc.gpsimd.dma_start(x1_bf[tc * 128:(tc + 1) * 128, :],
                                              xt[:, :]), 16)
            X1_DONE = s_dx1.n
            X1B_DONE = s_dxb.n

            if "x1" in tap_aps:
                nc.sync.wait_ge(s_dx1.h, X1_DONE)
                s_dst.inc(nc.sync.dma_start(tap_aps["x1"], x1d), 16)

        if upto >= 5:
            NG = 4
            wt_cnt = {0: 0, 1: 0}
            wt_need = {}

            def emit_x1fm(g):
                nc.sync.wait_ge(s_dxb.h, 16 * (4 * g + 4))
                if g >= 2:
                    nc.sync.wait_ge(s_zmm.h, 32 * g)
                for fc in range(8):
                    s_ft.inc(nc.sync.dma_start(
                        x1fm[g % 2][:, fc * 512:(fc + 1) * 512],
                        x1_bf[g * 512:(g + 1) * 512,
                              fc * 128:(fc + 1) * 128], transpose=True), 16)

            def emit_wt_load(p, ci):
                g, eh = divmod(p, 2)
                gci = p * 128 + ci
                slot = gci % 4
                q = gci % 2
                eng = nc.sync if q == 0 else nc.gpsimd
                if gci >= 4:
                    eng.wait_ge(s_cp.h, gci - 3)
                eng.wait_ge(s_ws.h, ws_need(eh, ci))
                sem = s_wt if q == 0 else s_wg
                sem.inc(eng.dma_start(
                    wt_ring[:, slot * 512:(slot + 1) * 512],
                    wt_dram[eh, ci]), 16)
                wt_need[gci] = (q, sem.n)

            def emit_z_group(g, m):
                base = PS_SC if g == 0 else PS_PV
                nc.tensor.wait_ge(s_ft.h, 16 * 8 * (g + 1))
                if m == 0:
                    nc.tensor.wait_ge(s_lz.h, LZ_DONE)
                    if g == 0:
                        nc.tensor.wait_ge(s_exp.h, EXP_ALL)
                    else:
                        nc.tensor.wait_ge(s_pvev.h, NQH)
                        nc.tensor.wait_ge(s_ev.h, 4 * g)
                nc.tensor.wait_ge(s_gl.h, 32 * g + m)
                for k in range(4):
                    fc = 4 * m + k
                    h_, p_ = fc // 32, fc % 32
                    row0, col0 = lz_blk(h_, p_)
                    mm_ = p_ // 4
                    mmi = nc.tensor.matmul(
                        ps[:, base + k * 512:base + (k + 1) * 512],
                        lz[row0:row0 + 64, col0:col0 + 128],
                        x1fm[g % 2][row0:row0 + 64, mm_ * 512:(mm_ + 1) * 512],
                        start=True, stop=True, skip_group_check=True)
                s_zmm.inc(mmi)

            def emit_gelu(g, m, cp_need=None):
                base = PS_SC if g == 0 else PS_PV
                nc.scalar.wait_ge(s_zmm.h, 32 * g + m + 1)
                if cp_need is not None:
                    nc.scalar.wait_ge(s_cp.h, cp_need)
                if g == 0 and m == 0:
                    nc.scalar.wait_ge(s_dxb.h, X1B_DONE)
                s_gl.inc(nc.scalar.activation(
                    h_ap(4 * m, 4), ps[:, base:base + 2048], AF.Gelu))

            def emit_cproj_ci(p, ci):
                g, eh = divmod(p, 2)
                base = PS_PV if eh == 0 else PS_SC
                gci = p * 128 + ci
                q, cnt = wt_need[gci]
                sem = s_wt if q == 0 else s_wg
                nc.tensor.wait_ge(sem.h, cnt)
                if eh == 0:
                    if g == 0:
                        nc.tensor.wait_ge(s_gl.h, ci // 4 + 1)
                        if ci == 0:
                            nc.tensor.wait_ge(s_pvev.h, NQH)
                    elif ci == 0:
                        nc.tensor.wait_ge(s_gl.h, 32 * (g + 1))
                else:
                    if ci == 0:
                        nc.tensor.wait_ge(s_gl.h, 32 * (g + 1))
                        if g >= 1:
                            nc.tensor.wait_ge(s_l2.h, 4 * g)
                slot = gci % 4
                for tjr in range(4):
                    mmi = nc.tensor.matmul(
                        ps[:, base + tjr * 512:base + (tjr + 1) * 512],
                        h_ap(ci)[:, tjr * 128:(tjr + 1) * 128],
                        wt_ring[:, slot * 512:(slot + 1) * 512],
                        start=(ci == 0), stop=(ci == CC - 1),
                        skip_group_check=True)
                s_cp.inc(mmi)

            def emit_evac_even(g):
                nc.vector.wait_ge(s_cp.h, (2 * g + 1) * CC)
                if g == 0:
                    nc.vector.wait_ge(s_dx1.h, X1_DONE)
                else:
                    nc.vector.wait_ge(s_l2.h, 4 * g)
                for t in range(4):
                    s_ev.inc(nc.vector.tensor_copy(
                        stage[:, t * 512:(t + 1) * 512],
                        ps[:, PS_PV + t * 512:PS_PV + (t + 1) * 512]))

            def emit_ln2(g):
                xi_marks = {}
                for t in range(4):
                    tile = 4 * g + t
                    xb = xin[t % 2]
                    if t >= 2:
                        nc.sync.wait_ge(s_l2.h, xi_marks[t - 2])
                    elif g >= 1:
                        nc.sync.wait_ge(s_l2.h, 4 * g)
                    nc.sync.wait_ge(s_dx1.h, X1_DONE)
                    s_xi.inc(nc.sync.dma_start(
                        xb[:, :], x1d[tile * 128:(tile + 1) * 128, :]), 16)
                    nc.vector.wait_ge(s_xi.h, s_xi.n)
                    if t == 0:
                        nc.vector.wait_ge(s_cp.h, (2 * g + 2) * CC)
                        nc.vector.wait_ge(s_ev.h, 4 * (g + 1))
                    if t >= 2:
                        nc.vector.wait_ge(s_ou.h, 16 * (4 * g + t - 1))
                    elif g >= 1:
                        nc.vector.wait_ge(s_ou.h, 16 * 4 * g)
                    oti = ot[t % 2]
                    if t >= 1:
                        nc.vector.wait_ge(s_ac.h, s_ac.n)
                    s_dv.inc(nc.vector.scalar_tensor_tensor(
                        oti[:, 0:512], stage[:, t * 512:(t + 1) * 512], 0.0,
                        xb[:, 0:512], ALU.add, ALU.add,
                        accum_out=stats[:, 64:65]))
                    dve_sync()
                    s_l2.inc(nc.vector.scalar_tensor_tensor(
                        oti[:, 512:1024],
                        ps[:, PS_SC + t * 512:PS_SC + (t + 1) * 512], 0.0,
                        xb[:, 512:1024], ALU.add, ALU.add,
                        accum_out=stats[:, 65:66]))
                    nc.vector.wait_ge(s_l2.h, s_l2.n)
                    xi_marks[t] = s_l2.n
                    nc.scalar.wait_ge(s_l2.h, s_l2.n)
                    s_ac.inc(nc.scalar.activation(
                        sq_scr[:, :], oti[:, :], AF.Square,
                        accum_out=stats[:, 66:67]))
                    nc.vector.wait_ge(s_ac.h, s_ac.n)
                    s_dv.inc(nc.vector.tensor_tensor(
                        stats[:, 68:69], stats[:, 64:65], stats[:, 65:66],
                        ALU.add))
                    dve_sync()
                    s_dv.inc(nc.vector.tensor_scalar(
                        stats[:, 67:68], stats[:, 68:69], -1.0 / D, None,
                        ALU.mult))
                    dve_sync()
                    s_dv.inc(nc.vector.tensor_tensor(
                        stats[:, 68:69], stats[:, 67:68], stats[:, 67:68],
                        ALU.mult))
                    dve_sync()
                    s_dv.inc(nc.vector.scalar_tensor_tensor(
                        stats[:, 68:69], stats[:, 66:67], 1.0 / D,
                        stats[:, 68:69], ALU.mult, ALU.subtract))
                    dve_sync()
                    nc.scalar.wait_ge(s_dv.h, s_dv.n)
                    s_ac.inc(nc.scalar.activation(
                        stats[:, 68:69], stats[:, 68:69], AF.Sqrt,
                        bias=cst_sb[:, C_EPS:C_EPS + 1]))
                    nc.vector.wait_ge(s_ac.h, s_ac.n)
                    s_dv.inc(nc.vector.reciprocal(stats[:, 69:70],
                                                  stats[:, 68:69]))
                    dve_sync()
                    s_dv.inc(nc.vector.tensor_scalar(
                        outt[t % 2][:, :], oti[:, :],
                        stats[:, 67:68], stats[:, 69:70], ALU.add, ALU.mult))
                    dve_sync()
                    nc.scalar.wait_ge(s_dv.h, s_dv.n)
                    s_ou.inc(nc.scalar.dma_start(
                        out[tile * 128:(tile + 1) * 128, :],
                        outt[t % 2][:, :]), 16)

            emit_x1fm(0)
            emit_x1fm(1)
            for gci in range(4):
                emit_wt_load(0, gci)
            for m in range(32):
                emit_z_group(0, m)
                emit_gelu(0, m)
                if m >= 1:
                    for ci in range(4 * (m - 1), 4 * m):
                        if ci + 4 < CC:
                            emit_wt_load(0, ci + 4)
                        emit_cproj_ci(0, ci)
            for ci in range(124, 128):
                emit_cproj_ci(0, ci)
            for g in range(NG):
                if g >= 1:
                    p = 2 * g
                    if g + 1 < NG:
                        emit_x1fm(g + 1)
                    for ci in range(CC):
                        if ci < 4:
                            emit_wt_load(p, ci)
                        if ci + 4 < CC:
                            emit_wt_load(p, ci + 4)
                        emit_cproj_ci(p, ci)
                emit_evac_even(g)
                p = 2 * g + 1
                cp_base = p * CC
                for ci in range(CC):
                    if ci < 4:
                        emit_wt_load(p, ci)
                    if ci + 4 < CC:
                        emit_wt_load(p, ci + 4)
                    emit_cproj_ci(p, ci)
                    if ci % 4 == 3 and g + 1 < NG:
                        m = ci // 4
                        emit_z_group(g + 1, m)
                        emit_gelu(g + 1, m, cp_need=cp_base + ci + 1)
                emit_ln2(g)

        if s_ou.n > 0:
            nc.sync.wait_ge(s_ou.h, s_ou.n)
            nc.scalar.wait_ge(s_ou.h, s_ou.n)
        if s_dst.n > 0:
            nc.sync.wait_ge(s_dst.h, s_dst.n)
            nc.gpsimd.wait_ge(s_dst.h, s_dst.n)

    return nc


def _consts():
    cst = np.zeros((128, 8), dtype=np.float32)
    cst[:, C_EXPB] = -4.0
    cst[:, C_EPS] = 1e-5
    return cst


def _in_map(inputs, b):
    a = {k: np.asarray(v) for k, v in inputs.items()}
    return {
        "x": np.ascontiguousarray(a["L0_attn_x"][b]),
        "wq": a["L0_attn_Wq"], "wk": a["L0_attn_Wk"], "wv": a["L0_attn_Wv"],
        "mem": a["L0_attn_memory"],
        "memn": a["L0_attn_memory_norm"],
        "memw": a["L0_attn_memory_weight"],
        "rm": np.ascontiguousarray(a["L0_ffn_rail_maker"].reshape(4096, 16)),
        "rup": np.ascontiguousarray(a["L0_ffn_rail_up_proj"].reshape(16384, 16)),
        "wc": a["L0_ffn_w_cproj"],
        "cst": _consts(),
    }


def kernel(**inputs):
    nc = build(upto=99)
    in_maps = [_in_map(inputs, b) for b in range(8)]
    res = run_bass_kernel_spmd(nc, in_maps, core_ids=list(range(8)))
    return np.stack([np.asarray(r["out"]) for r in res.results], axis=0)


def kernel_dev(inputs, upto=99, taps=(), cores=(0,), dbg_dram=False):
    nc = build(upto=upto, taps=taps, dbg_dram=dbg_dram)
    in_maps = [_in_map(inputs, b) for b in cores]
    res = run_bass_kernel_spmd(nc, in_maps, core_ids=list(range(len(cores))))
    return res.results
